# version 29
# baseline (speedup 1.0000x reference)
import math
import numpy as np
import ml_dtypes
import concourse.bass as bass
import concourse.bass_isa as bass_isa
import concourse.mybir as mybir
from concourse.bass_utils import run_bass_kernel_spmd

F32 = mybir.dt.float32
BF16 = mybir.dt.bfloat16
FP8 = mybir.dt.float8e4
I8 = mybir.dt.int8
I16 = mybir.dt.int16
AX = mybir.AxisListType.X
AF = mybir.ActivationFunctionType
OP = mybir.AluOpType

C = 1024
NH = 16
HS = 64
KVL = 256
QL = 256
NEG = -1.0e9
EPS = 1e-5
INV_SQRT_HS = 0.125

class Op:
    __slots__ = ("i", "eng", "fn", "deps", "is_dma", "inc", "tag")
    def __init__(self, i, eng, fn, deps, is_dma, tag):
        self.i, self.eng, self.fn, self.deps = i, eng, fn, deps
        self.is_dma, self.inc, self.tag = is_dma, False, tag

class Sched:
    ENGINES = ("tensor", "vector", "scalar", "gpsimd", "sync")

    def __init__(self):
        self.ops = []
        self.writer = {}
        self.readers = {}

    def emit(self, eng, fn, reads=(), writes=(), extra=(), is_dma=False, tag=""):
        deps = set(d.i for d in extra if d is not None)
        for k in reads:
            w = self.writer.get(k)
            if w is not None:
                deps.add(w)
        for k in writes:
            w = self.writer.get(k)
            if w is not None:
                deps.add(w)
            for r in self.readers.get(k, ()):
                deps.add(r)
        op = Op(len(self.ops), eng, fn, sorted(deps), is_dma, tag)
        self.ops.append(op)
        for k in reads:
            self.readers.setdefault(k, []).append(op.i)
        for k in writes:
            self.writer[k] = op.i
            self.readers[k] = []
        return op

    def finalize(self, nc, block, sems):
        ops = self.ops
        for op in ops:
            for d in op.deps:
                dep = ops[d]
                if dep.eng != op.eng or dep.is_dma or dep.eng != "tensor":
                    dep.inc = True
        for op in ops:
            if op.is_dma:
                op.inc = True
        comp_count = {}
        dma_count = {}
        run_comp = {e: 0 for e in self.ENGINES}
        run_dma = {e: 0 for e in self.ENGINES}
        for op in ops:
            if op.is_dma:
                run_dma[op.eng] += 1
                dma_count[op.i] = run_dma[op.eng]
            elif op.inc:
                run_comp[op.eng] += 1
                comp_count[op.i] = run_comp[op.eng]
        n = len(ops)
        prefix = {e: [0] * (n + 1) for e in self.ENGINES}
        for e in self.ENGINES:
            p = prefix[e]
            for i, op in enumerate(ops):
                p[i + 1] = p[i] + (1 if (op.is_dma and op.eng == e) else 0)
        waits = {i: {} for i in range(n)}
        for op in ops:
            for d in op.deps:
                dep = ops[d]
                if dep.is_dma:
                    sem = "dq_" + dep.eng
                    val = 16 * prefix[dep.eng][op.i]
                    assert val >= 16 * dma_count[dep.i]
                elif dep.eng == op.eng and dep.eng == "tensor":
                    continue
                else:
                    sem = "c_" + dep.eng
                    val = comp_count[dep.i]
                w = waits[op.i]
                if w.get(sem, -1) < val:
                    w[sem] = val
        for e in self.ENGINES:
            bounds = sorted({v for i_, w in waits.items() for s_, v in w.items()
                             if s_ == "dq_" + e})
            if not bounds:
                continue
            k = 0
            for op in ops:
                if not (op.is_dma and op.eng == e):
                    continue
                gate = 0
                for b in bounds:
                    if b <= 16 * k:
                        gate = b
                    else:
                        break
                if gate:
                    w = waits[op.i]
                    sem = "dq_" + e
                    if w.get(sem, -1) < gate:
                        w[sem] = gate
                k += 1
        per_eng = {e: [op for op in ops if op.eng == e] for e in self.ENGINES}

        def emit_engine(eng_name, eng):
            waited = {}
            for op in per_eng[eng_name]:
                for sem_name, val in sorted(waits[op.i].items()):
                    if waited.get(sem_name, -1) >= val:
                        continue
                    eng.wait_ge(sems[sem_name], val)
                    waited[sem_name] = val
                inst = op.fn(eng)
                if op.inc:
                    if op.is_dma:
                        inst.then_inc(sems["dq_" + op.eng], 16)
                    else:
                        assert inst is not None, f"op {op.tag} returned no inst"
                        inst.then_inc(sems["c_" + op.eng], 1)
            if eng_name == "sync":
                eng.wait_ge(sems["dq_sync"], 16 * prefix["sync"][n])
            if eng_name == "gpsimd" and prefix["gpsimd"][n]:
                eng.wait_ge(sems["dq_gpsimd"], 16 * prefix["gpsimd"][n])
            if eng_name == "scalar" and prefix["scalar"][n]:
                eng.wait_ge(sems["dq_scalar"], 16 * prefix["scalar"][n])

        @block.tensor
        def _(e):
            emit_engine("tensor", e)

        @block.vector
        def _(e):
            emit_engine("vector", e)

        @block.scalar
        def _(e):
            emit_engine("scalar", e)

        @block.gpsimd
        def _(e):
            emit_engine("gpsimd", e)

        @block.sync
        def _(e):
            emit_engine("sync", e)


def build_program(T=2048):
    NT = T // 128
    NW = T // 512
    CC = C // 128
    POOLW = NT * 512

    nc = bass.Bass("TRN2", target_bir_lowering=False)
    d_x = nc.dram_tensor("xT", [C, T], F32, kind="ExternalInput")
    d_wdq = nc.dram_tensor("W_dq", [QL, C], F32, kind="ExternalInput")
    d_wuq = nc.dram_tensor("W_uq", [C, QL], F32, kind="ExternalInput")
    d_wdkv = nc.dram_tensor("W_dkv", [KVL, C], F32, kind="ExternalInput")
    d_wuk = nc.dram_tensor("W_uk", [C, KVL], F32, kind="ExternalInput")
    d_wuv = nc.dram_tensor("W_uv", [C, KVL], F32, kind="ExternalInput")
    d_wo = nc.dram_tensor("W_o", [C, C], F32, kind="ExternalInput")
    d_wk = nc.dram_tensor("Wk", [C, C], BF16, kind="ExternalInput")
    d_sin = nc.dram_tensor("s_in", [128, 1], F32, kind="ExternalInput")
    d_sk = nc.dram_tensor("s_k", [128, 1], F32, kind="ExternalInput")
    d_bias = nc.dram_tensor("fbias", [128, C // 128], BF16, kind="ExternalInput")
    d_eye = nc.dram_tensor("eye128", [128, 128], BF16, kind="ExternalInput")
    d_eyef = nc.dram_tensor("eyef128", [128, 128], F32, kind="ExternalInput")
    d_mask = nc.dram_tensor("maskqk", [128, 128], BF16, kind="ExternalInput")
    d_ones = nc.dram_tensor("onesm", [128, 128], BF16, kind="ExternalInput")
    d_onesr = nc.dram_tensor("onesrow", [1, 128], F32, kind="ExternalInput")
    d_out = nc.dram_tensor("out", [C, T], F32, kind="ExternalOutput")

    from contextlib import ExitStack
    with ExitStack() as stack:
        E_ = stack.enter_context
        xs = E_(nc.sbuf_tensor("xs", [128, CC, T], F32))
        eyeb = E_(nc.sbuf_tensor("eyeb", [128, 128], BF16))
        maskb = E_(nc.sbuf_tensor("maskb", [128, 128], BF16))
        onesm = E_(nc.sbuf_tensor("onesm_s", [128, 128], BF16))
        eyef = E_(nc.sbuf_tensor("eyef_s", [128, 128], F32))
        onesr = E_(nc.sbuf_tensor("onesr_s", [1, 128], F32))
        wdq_b = E_(nc.sbuf_tensor("wdq_b", [128, 2, C], BF16))
        wuq_b = E_(nc.sbuf_tensor("wuq_b", [128, CC, QL], BF16))
        wdkv_b = E_(nc.sbuf_tensor("wdkv_b", [128, 2, C], BF16))
        wuk_b = E_(nc.sbuf_tensor("wuk_b", [128, CC, KVL], BF16))
        wuv_b = E_(nc.sbuf_tensor("wuv_b", [128, CC, KVL], BF16))
        wdqT = E_(nc.sbuf_tensor("wdqT", [128, CC, QL], BF16))
        wdkvT = E_(nc.sbuf_tensor("wdkvT", [128, CC, KVL], BF16))
        keff = E_(nc.sbuf_tensor("keff", [128, CC, KVL], BF16))
        emat = E_(nc.sbuf_tensor("emat", [128, 2, NH * KVL], BF16))
        veff = E_(nc.sbuf_tensor("veff", [128, 2, C], BF16))
        kq8 = E_(nc.sbuf_tensor("kq8", [128, CC, C], FP8))
        wo_st = E_(nc.sbuf_tensor("wo_st", [128, 2, C], F32))
        wk_st = E_(nc.sbuf_tensor("wk_st", [128, 1, C], BF16))
        xb_st = E_(nc.sbuf_tensor("xb_st", [128, 2, 512], BF16))
        cq = E_(nc.sbuf_tensor("cq", [128, 2, T], BF16))
        ckvT = E_(nc.sbuf_tensor("ckvT", [128, 2, T], BF16))
        ckvkm = E_(nc.sbuf_tensor("ckvkm", [128, NT, 2, 128], BF16))
        qlat = E_(nc.sbuf_tensor("qlat", [128, 2, T], BF16))
        pool = E_(nc.sbuf_tensor("pool", [128, NT, 512], BF16))
        ctxs = E_(nc.sbuf_tensor("ctxs", [128, 2, 512], BF16))
        dmax = E_(nc.sbuf_tensor("dmax", [128, 16], F32))
        gmax = E_(nc.sbuf_tensor("gmax", [128, 1], F32))
        armax = E_(nc.sbuf_tensor("armax", [128, 1], F32))
        negB = E_(nc.sbuf_tensor("negB", [128, 2], F32))
        rbc = E_(nc.sbuf_tensor("rbc", [128, 512], F32))
        sc_st = E_(nc.sbuf_tensor("sc_st", [1, 4], F32))
        sinb = E_(nc.sbuf_tensor("sinb", [128, 1], F32))
        skb = E_(nc.sbuf_tensor("skb", [128, 1], F32))
        rinb = E_(nc.sbuf_tensor("rinb", [128, 1], F32))
        rkb = E_(nc.sbuf_tensor("rkb", [128, 1], F32))
        sscale = E_(nc.sbuf_tensor("sscale", [128, 1], F32))
        bias_st = E_(nc.sbuf_tensor("bias_st", [128, CC], BF16))
        biasf = E_(nc.sbuf_tensor("biasf", [128, CC], F32))
        nmax = E_(nc.sbuf_tensor("nmax", [1, 1], F32))
        mtile = E_(nc.sbuf_tensor("mtile", [128, 512], F32))
        rtile = E_(nc.sbuf_tensor("rtile", [128, 512], F32))
        t_a = E_(nc.sbuf_tensor("t_a", [128, 512], F32))
        t_b = E_(nc.sbuf_tensor("t_b", [128, 512], F32))
        xbln = E_(nc.sbuf_tensor("xbln", [128, 2, 512], BF16))
        xq_st = E_(nc.sbuf_tensor("xq_st", [128, CC, 512], FP8))
        tst = E_(nc.sbuf_tensor("tst", [128, 2, 512], BF16))

        ping = E_(nc.psum_tensor("ping", [128, 1024], F32))
        pong = E_(nc.psum_tensor("pong", [128, 1024], F32))
        ctxps = E_(nc.psum_tensor("ctxps", [128, 2, 512], F32))
        rsps = E_(nc.psum_tensor("rsps", [128, 512], F32))
        auxps = E_(nc.psum_tensor("auxps", [128, 512], F32))

        sems = {}
        for e in Sched.ENGINES:
            sems["c_" + e] = E_(nc.semaphore("c_" + e))
        sems["dq_sync"] = E_(nc.semaphore("dq_sync"))
        sems["dq_gpsimd"] = E_(nc.semaphore("dq_gpsimd"))
        sems["dq_scalar"] = E_(nc.semaphore("dq_scalar"))

        S = Sched()

        S.emit("sync", lambda e: e.dma_start(out=eyeb[:, :], in_=d_eye[:, :]),
               writes=["eyeb"], is_dma=True)
        S.emit("sync", lambda e: e.dma_start(out=maskb[:, :], in_=d_mask[:, :]),
               writes=["maskb"], is_dma=True)
        S.emit("sync", lambda e: e.dma_start(out=onesm[:, :], in_=d_ones[:, :]),
               writes=["onesm"], is_dma=True)
        S.emit("sync", lambda e: e.dma_start(out=eyef[:, :], in_=d_eyef[:, :]),
               writes=["eyef"], is_dma=True)
        S.emit("sync", lambda e: e.dma_start(out=onesr[:, :], in_=d_onesr[:, :]),
               writes=["onesr"], is_dma=True)
        S.emit("sync", lambda e: e.dma_start(out=bias_st[:, :], in_=d_bias[:, :]),
               writes=["bias_st"], is_dma=True)
        S.emit("sync", lambda e: e.dma_start(out=sinb[:, :], in_=d_sin[:, :]),
               writes=["sinb"], is_dma=True)
        S.emit("sync", lambda e: e.dma_start(out=skb[:, :], in_=d_sk[:, :]),
               writes=["skb"], is_dma=True)
        W5 = ((d_wdq, wdq_b, "wdq_b"), (d_wuq, wuq_b, "wuq_b"),
              (d_wdkv, wdkv_b, "wdkv_b"), (d_wuk, wuk_b, "wuk_b"),
              (d_wuv, wuv_b, "wuv_b"))
        if T >= 2048:
            for i, (dsrc, dst, dkey) in enumerate(W5):
                def f(e, dsrc=dsrc, i=i):
                    return e.dma_start(
                        out=xs[:, i, 0:2048].rearrange("p (a c) -> p a c", a=2048 // C),
                        in_=dsrc[:, :].rearrange("(a p) c -> p a c", p=128))
                S.emit("sync", f, writes=["xs%d" % i], is_dma=True, tag="ldw%d" % i)
                S.emit("vector", lambda e, dst=dst, i=i: e.tensor_copy(
                    dst[:, :, :], xs[:, i, 0:2048]),
                    reads=["xs%d" % i], writes=[dkey], tag="castw%d" % i)
        else:
            for i, (dsrc, dst, dkey) in enumerate(W5):
                def f(e, dsrc=dsrc, dst=dst):
                    return e.dma_start(
                        out=dst[:, :, :],
                        in_=dsrc[:, :].rearrange("(a p) c -> p a c", p=128))
                S.emit("gpsimd", f, writes=[dkey], is_dma=True, tag="ldw%d" % i)

        def emit_kq8():
            for cr in range(CC):
                S.emit("sync", lambda e, cr=cr: e.dma_start(
                    out=wk_st[:, 0, :], in_=d_wk[128 * cr:128 * cr + 128, :]),
                    writes=["wk"], is_dma=True, tag="ld_wk%d" % cr)
                S.emit("vector", lambda e, cr=cr: e.tensor_scalar(
                    out=kq8[:, cr, :], in0=wk_st[:, 0, :], scalar1=rkb[:, :],
                    scalar2=None, op0=OP.mult),
                    reads=["wk", "rkb"], writes=["kq8"], tag="kq8")
            S.emit("scalar", lambda e: e.dma_start(
                out=kq8[:, :, :].bitcast(I8), in_=kq8[:, :, :].bitcast(I8)),
                reads=["kq8"], writes=["kq8"], is_dma=True, tag="kq8sync")

        S.emit("vector", lambda e: e.reciprocal(rinb[:, :], sinb[:, :]),
               reads=["sinb"], writes=["rinb"])
        S.emit("vector", lambda e: e.reciprocal(rkb[:, :], skb[:, :]),
               reads=["skb"], writes=["rkb"])
        S.emit("vector", lambda e: e.tensor_tensor(sscale[:, :], sinb[:, :], skb[:, :], op=OP.mult),
               reads=["sinb", "skb"], writes=["sscale"])
        S.emit("vector", lambda e: e.tensor_copy(biasf[:, :], bias_st[:, :]),
               reads=["bias_st"], writes=["biasf"])
        emit_kq8()

        def gen_u_keff():
            for m in range(2):
                def f(e, m=m):
                    for cc_ in range(CC):
                        inst = e.matmul(ctxps[:, m, 0:KVL],
                                        wuq_b[:, cc_, 128 * m:128 * m + 128],
                                        wuk_b[:, cc_, :],
                                        start=(cc_ == 0), stop=(cc_ == CC - 1))
                    return inst
                S.emit("tensor", f, reads=["wuq_b", "wuk_b"], writes=["ctx%d" % m], tag="U")
                S.emit("vector", lambda e, m=m: e.tensor_copy(ctxs[:, m, 0:KVL], ctxps[:, m, 0:KVL]),
                       reads=["ctx%d" % m], writes=["ctxs%d" % m])
                yield
            for mm_ in range(CC):
                def f(e, mm_=mm_):
                    for qlc in range(2):
                        inst = e.matmul(ctxps[:, mm_ % 2, 0:KVL],
                                        wdq_b[:, qlc, 128 * mm_:128 * mm_ + 128],
                                        ctxs[:, qlc, 0:KVL],
                                        start=(qlc == 0), stop=(qlc == 1))
                    return inst
                S.emit("tensor", f, reads=["wdq_b", "ctxs0", "ctxs1"], writes=["ctx%d" % (mm_ % 2)], tag="keff")
                S.emit("vector", lambda e, mm_=mm_: e.tensor_copy(
                    keff[:, mm_, :], ctxps[:, mm_ % 2, 0:KVL]),
                    reads=["ctx%d" % (mm_ % 2)], writes=["keff"])
                yield

        def gen_emat():
            for _ in range(12):
                yield
            for h in range(NH):
                po = (h % 2) * 64
                hc = h // 2
                for m in range(2):
                    def f(e, po=po, hc=hc, m=m):
                        return e.matmul(auxps[:, 0:KVL],
                                        wuq_b[po:po + 64, hc, 128 * m:128 * m + 128],
                                        keff[po:po + 64, hc, :],
                                        start=True, stop=True)
                    S.emit("tensor", f, reads=["wuq_b", "keff"], writes=["auxps"], tag="E")
                    S.emit("vector", lambda e, h=h, m=m: e.tensor_copy(
                        emat[:, m, KVL * h:KVL * (h + 1)], auxps[:, 0:KVL]),
                        reads=["auxps"], writes=["emat"])
                    yield

        def gen_wt(stage_idx, bsrc, bskey, dst, dkey, stage, stkey):
            for qlc in range(2):
                for cc_ in range(CC):
                    if T >= 2048:
                        def f(e, stage_idx=stage_idx, qlc=qlc, cc_=cc_, stage=stage):
                            return e.transpose(
                                stage[:, 0:128],
                                xs[:, stage_idx, 1024 * qlc + 128 * cc_:1024 * qlc + 128 * cc_ + 128],
                                eyef[:, :])
                        S.emit("tensor", f, reads=["xs%d" % stage_idx, "eyef"],
                               writes=[stkey], tag="wT")
                        S.emit("vector", lambda e, dst=dst, cc_=cc_, qlc=qlc, stage=stage: e.tensor_copy(
                            dst[:, cc_, 128 * qlc:128 * qlc + 128], stage[:, 0:128]),
                            reads=[stkey], writes=[dkey])
                    else:
                        def f(e, bsrc=bsrc, qlc=qlc, cc_=cc_, stage=stage):
                            return e.transpose(stage[:, 0:64].bitcast(BF16),
                                               bsrc[:, qlc, 128 * cc_:128 * cc_ + 128],
                                               eyeb[:, :])
                        S.emit("tensor", f, reads=[bskey, "eyeb"], writes=[stkey], tag="wT")
                        S.emit("vector", lambda e, dst=dst, cc_=cc_, qlc=qlc, stage=stage: e.tensor_copy(
                            dst[:, cc_, 128 * qlc:128 * qlc + 128], stage[:, 0:64].bitcast(BF16)),
                            reads=[stkey], writes=[dkey])
                    yield

        def gen_veff():
            def ld(cr):
                S.emit("gpsimd", lambda e, cr=cr: e.dma_start(
                    out=wo_st[:, cr % 2, :], in_=d_wo[128 * cr:128 * cr + 128, :]),
                    writes=["wo%d" % (cr % 2)], is_dma=True, tag="ld_wo%d" % cr)
            ld(0)
            for cr in range(CC):
                if cr + 1 < CC:
                    ld(cr + 1)
                for jc in range(CC):
                    def f(e, cr=cr, jc=jc):
                        return e.transpose(pong[:, 0:128],
                                           wo_st[:, cr % 2, 128 * jc:128 * jc + 128],
                                           eyef[:, :])
                    S.emit("tensor", f, reads=["wo%d" % (cr % 2), "eyef"],
                           writes=["pong"], tag="woT")
                    S.emit("vector", lambda e, jc=jc: e.tensor_copy(
                        pool[:, jc // 4, 128 * (jc % 4):128 * (jc % 4) + 128],
                        pong[:, 0:128]),
                        reads=["pong"], writes=["pw%d" % (jc // 4)])
                    if jc % 4 == 3:
                        yield
                for m in range(2):
                    def f(e, m=m):
                        for jc in range(CC):
                            inst = e.matmul(pong[:, 512 + 128 * m:512 + 128 * m + 128],
                                            wuv_b[:, jc, 128 * m:128 * m + 128],
                                            pool[:, jc // 4, 128 * (jc % 4):128 * (jc % 4) + 128],
                                            start=(jc == 0), stop=(jc == CC - 1))
                        return inst
                    S.emit("tensor", f, reads=["wuv_b", "pw0", "pw1"],
                           writes=["pong"], tag="veff")
                    S.emit("vector", lambda e, m=m, cr=cr: e.tensor_copy(
                        veff[:, m, 128 * cr:128 * cr + 128],
                        pong[:, 512 + 128 * m:512 + 128 * m + 128]),
                        reads=["pong"], writes=["veff"])
                    yield

        gens = [gen_u_keff(), gen_emat(),
                gen_wt(0, wdq_b, "wdq_b", wdqT, "wdqT", rsps, "rsps"),
                gen_wt(2, wdkv_b, "wdkv_b", wdkvT, "wdkvT", ping, "ping"),
                gen_veff()]
        while gens:
            alive = []
            for g in gens:
                try:
                    next(g)
                    alive.append(g)
                except StopIteration:
                    continue
            gens = alive

        for w in range(NW):
            sl = slice(512 * w, 512 * w + 512)
            def fx(e, sl=sl):
                return e.dma_start(
                    out=xs[:, :, sl],
                    in_=d_x[:, :].rearrange("(c p) t -> p c t", p=128)[:, :, sl])
            S.emit("scalar" if w % 2 == 0 else "sync", fx,
                   writes=["xs%d" % c for c in range(CC)], is_dma=True,
                   tag="ld_x w%d" % w)
            for cc_ in range(CC):
                S.emit("vector", lambda e, cc_=cc_, sl=sl: e.tensor_copy(
                    xb_st[:, cc_ % 2, :], xs[:, cc_, sl]),
                    reads=["xs%d" % cc_], writes=["xb%d" % (cc_ % 2)], tag="xbcast")
                for qlc in range(2):
                    def f(e, cc_=cc_, qlc=qlc):
                        return e.matmul(ping[:, 512 * qlc:512 * qlc + 512],
                                        wdqT[:, cc_, 128 * qlc:128 * qlc + 128],
                                        xb_st[:, cc_ % 2, :],
                                        start=(cc_ == 0), stop=(cc_ == CC - 1))
                    S.emit("tensor", f, reads=["wdqT", "xb%d" % (cc_ % 2)],
                           writes=["ping"], tag="cq")
                for lc in range(2):
                    def f(e, cc_=cc_, lc=lc):
                        return e.matmul(pong[:, 512 * lc:512 * lc + 512],
                                        wdkvT[:, cc_, 128 * lc:128 * lc + 128],
                                        xb_st[:, cc_ % 2, :],
                                        start=(cc_ == 0), stop=(cc_ == CC - 1))
                    S.emit("tensor", f, reads=["wdkvT", "xb%d" % (cc_ % 2)],
                           writes=["pong"], tag="ckv")
            for qlc in range(2):
                S.emit("vector", lambda e, qlc=qlc, sl=sl: e.tensor_copy(
                    cq[:, qlc, sl], ping[:, 512 * qlc:512 * qlc + 512]),
                    reads=["ping"], writes=["cq"])
            for lc in range(2):
                S.emit("vector", lambda e, lc=lc, sl=sl: e.tensor_copy(
                    ckvT[:, lc, sl], pong[:, 512 * lc:512 * lc + 512]),
                    reads=["pong"], writes=["ckvT"])

        all_dma = [op for op in S.ops if op.is_dma]
        for lc in range(2):
            def f(e, lc=lc):
                return e.dma_start_transpose(out=ckvkm[:, :, lc, :], in_=ckvT[:, lc, 0:T])
            S.emit("sync", f, reads=["ckvT"], writes=["ckvkm"],
                   extra=all_dma, is_dma=True, tag="ckvkm_T")

        rot = [0]

        def score_slot():
            i = rot[0] % 4
            rot[0] += 1
            buf = ping if i < 2 else pong
            off = 512 * (i % 2)
            return buf, off, ("ping" if i < 2 else "pong")

        def emit_qlat_diag(h, first=False):
            nhalf = NT // 8 + (1 if NT % 8 else 0)
            for w in range(NW):
                sl = slice(512 * w, 512 * w + 512)
                for lc in range(2):
                    buf, off, bkey = score_slot()
                    def f(e, h=h, lc=lc, sl=sl, buf=buf, off=off):
                        for qlc in range(2):
                            inst = e.matmul(
                                buf[:, off:off + 512],
                                emat[:, qlc, KVL * h + 128 * lc:KVL * h + 128 * lc + 128],
                                cq[:, qlc, sl],
                                start=(qlc == 0), stop=(qlc == 1))
                        return inst
                    S.emit("tensor", f, reads=["emat", "cq"], writes=[bkey],
                           tag="qlat h%d" % h)
                    S.emit("vector", lambda e, lc=lc, sl=sl, buf=buf, off=off: e.tensor_copy(
                        qlat[:, lc, sl], buf[:, off:off + 512]),
                        reads=[bkey], writes=["qlat"], tag="qlat_ev")
                dbuf, dkey = (ping, "ping") if (w // 2) % 2 == 0 else (pong, "pong")
                doff = 512 * (w % 2)
                def f(e, h=h, w=w, dbuf=dbuf, doff=doff):
                    for t_ in range(4):
                        kt = 4 * w + t_
                        for lc in range(2):
                            e.matmul(dbuf[:, doff + 128 * t_:doff + 128 * t_ + 128],
                                     ckvT[:, lc, 128 * kt:128 * kt + 128],
                                     qlat[:, lc, 128 * kt:128 * kt + 128],
                                     start=(lc == 0), stop=False)
                        inst = e.matmul(dbuf[:, doff + 128 * t_:doff + 128 * t_ + 128],
                                        maskb[:, :], eyeb[:, :],
                                        start=False, stop=True)
                    return inst
                S.emit("tensor", f, reads=["ckvT", "qlat", "maskb", "eyeb"],
                       writes=[dkey], tag="diag h%d w%d" % (h, w))
                def f(e, w=w, dbuf=dbuf, doff=doff):
                    return e.tensor_reduce(
                        dmax[:, 4 * w:4 * w + 4],
                        dbuf[:, doff:doff + 512].rearrange("p (t k) -> p t k", k=128),
                        axis=AX, op=OP.max)
                S.emit("vector", f, reads=[dkey], writes=["dmax"], tag="dmaxr")
            S.emit("vector", lambda e: e.tensor_reduce(
                gmax[:, :], dmax[:, 0:NT], axis=AX, op=OP.max),
                reads=["dmax"], writes=["gmax"])

        def emit_negb(h):
            hp = h % 2
            S.emit("tensor", lambda e: e.transpose(
                ctxps[0:1, 1, 0:128], gmax[:, :], eyef[:, :]),
                reads=["gmax", "eyef"], writes=["ctx1"], tag="gmaxT")
            S.emit("vector", lambda e: e.tensor_reduce(
                nmax[:, :], ctxps[0:1, 1, 0:128], axis=AX, op=OP.max, negate=True),
                reads=["ctx1"], writes=["nmax"], tag="nmaxr")
            S.emit("tensor", lambda e: e.matmul(
                ctxps[:, 0, 0:1], onesr[:, :], nmax[:, :], start=True, stop=True),
                reads=["onesr", "nmax"], writes=["ctx0"], tag="negBmm")
            S.emit("vector", lambda e, hp=hp: e.tensor_copy(
                negB[:, hp:hp + 1], ctxps[:, 0, 0:1]),
                reads=["ctx0"], writes=["negB%d" % hp], tag="negB")

        def emit_window(h, w, negb_hook=None, prev_y=None):
            hp = h % 2
            wsl = slice(512 * w, 512 * w + 512)
            kcs = list(range(min(4 * w + 4, NT)))
            kc_full = [kc for kc in kcs if 128 * kc <= 512 * w]
            kc_part = [kc for kc in kcs if kc not in kc_full]
            groups = [kc_full[i:i + 2] for i in range(0, len(kc_full), 2)]
            groups += [[kc] for kc in kc_part]

            def emit_group(grp, pre_exp=None):
                if len(grp) == 2 and rot[0] % 2 == 1:
                    rot[0] += 1
                slots = []
                for kc in grp:
                    q0 = max(512 * w, 128 * kc)
                    n = 512 * (w + 1) - q0
                    buf, off, bkey = score_slot()
                    slots.append((kc, q0, n, buf, off, bkey))
                    has_diag = (q0 == 128 * kc)
                    def f(e, kc=kc, q0=q0, n=n, buf=buf, off=off, has_diag=has_diag):
                        for lc in range(2):
                            inst = e.matmul(buf[:, off:off + n],
                                            ckvT[:, lc, 128 * kc:128 * kc + 128],
                                            qlat[:, lc, q0:q0 + n],
                                            start=(lc == 0),
                                            stop=(lc == 1 and not has_diag))
                        if has_diag:
                            inst = e.matmul(buf[:, off:off + 128],
                                            maskb[:, :], eyeb[:, :],
                                            start=False, stop=True)
                        return inst
                    S.emit("tensor", f, reads=["ckvT", "qlat", "maskb", "eyeb"],
                           writes=[bkey], tag="sT h%d w%d kc%d" % (h, w, kc))
                if pre_exp is not None:
                    pre_exp()
                if (len(slots) == 2 and slots[0][3] is slots[1][3]
                        and slots[0][4] + 512 == slots[1][4]
                        and slots[0][2] == 512 and slots[1][2] == 512
                        and slots[1][0] == slots[0][0] + 1):
                    kc, q0, n, buf, off, bkey = slots[0]
                    def f(e, kc=kc, buf=buf, off=off, hp=hp):
                        return e.activation(
                            out=pool[:, kc:kc + 2, :].rearrange("p a b -> p (a b)"),
                            in_=buf[:, off:off + 1024],
                            func=AF.Exp, bias=negB[:, hp:hp + 1], scale=INV_SQRT_HS)
                    S.emit("scalar", f,
                           reads=[bkey, slots[1][5], "negB%d" % hp],
                           writes=["pw%d" % kc, "pw%d" % (kc + 1)],
                           tag="exp2 h%d w%d kc%d" % (h, w, kc))
                else:
                    for (kc, q0, n, buf, off, bkey) in slots:
                        def f(e, kc=kc, n=n, buf=buf, off=off, hp=hp):
                            return e.activation(
                                out=pool[:, kc, 0:n], in_=buf[:, off:off + n],
                                func=AF.Exp, bias=negB[:, hp:hp + 1], scale=INV_SQRT_HS)
                        S.emit("scalar", f, reads=[bkey, "negB%d" % hp],
                               writes=["pw%d" % kc],
                               tag="exp h%d w%d kc%d" % (h, w, kc))

            def emit_consume(kc, first, last):
                q0 = max(512 * w, 128 * kc)
                n = 512 * (w + 1) - q0
                co = q0 - 512 * w
                def f(e, kc=kc, n=n, co=co, first=first, last=last):
                    return e.matmul(rsps[:, co:co + n], onesm[:, :],
                                    pool[:, kc, 0:n], start=first, stop=last)
                S.emit("tensor", f, reads=["onesm", "pw%d" % kc],
                       writes=["rsps"], tag="rsum h%d w%d kc%d" % (h, w, kc))
                for lc in range(2):
                    def f(e, kc=kc, n=n, co=co, lc=lc, first=first, last=last):
                        return e.matmul(ctxps[:, lc, co:co + n],
                                        ckvkm[:, kc, lc, :],
                                        pool[:, kc, 0:n], start=first, stop=last)
                    S.emit("tensor", f, reads=["ckvkm", "pw%d" % kc],
                           writes=["ctx%d" % lc], tag="ctx h%d w%d kc%d" % (h, w, kc))

            def first_hook():
                if negb_hook is not None:
                    negb_hook()
                if prev_y is not None:
                    prev_y()
            for j, grp in enumerate(groups):
                emit_group(grp, pre_exp=(first_hook if j == 0 else None))
                if j >= 2:
                    for kc in groups[j - 2]:
                        emit_consume(kc, first=(kc == kcs[0]), last=(kc == kcs[-1]))
            for grp in groups[-2:] if len(groups) >= 2 else groups:
                for kc in grp:
                    emit_consume(kc, first=(kc == kcs[0]), last=(kc == kcs[-1]))
            for lc in range(2):
                S.emit("scalar", lambda e, lc=lc: e.copy(
                    ctxs[:, lc, :], ctxps[:, lc, :]),
                    reads=["ctx%d" % lc], writes=["ctxs%d" % lc],
                    tag="ctxev")
            po = (h % 2) * 64
            hc = h // 2
            def emit_y():
                S.emit("vector", lambda e: e.reciprocal(rbc[:, :], rsps[:, 0:512]),
                       reads=["rsps"], writes=["rbc"], tag="recip")
                def f(e, h=h, po=po):
                    for lc in range(2):
                        inst = e.matmul(auxps[po:po + 64, 0:512],
                                        veff[:, lc, 64 * h:64 * h + 64],
                                        ctxs[:, lc, :],
                                        start=(lc == 0), stop=(lc == 1),
                                        tile_position=(0, po) if po else None)
                    return inst
                S.emit("tensor", f, reads=["veff", "ctxs0", "ctxs1"],
                       writes=["auxps"], tag="y h%d w%d" % (h, w))
                def f(e, po=po):
                    return e.tensor_tensor(
                        t_b[po:po + 64, :], auxps[po:po + 64, 0:512],
                        rbc[po:po + 64, :], op=OP.mult)
                S.emit("vector", f, reads=["auxps", "rbc"], writes=["t_b"],
                       tag="yscale h%d w%d" % (h, w))
                def f(e, po=po, hc=hc, wsl=wsl):
                    return e.tensor_tensor(
                        xs[po:po + 64, hc, wsl], xs[po:po + 64, hc, wsl],
                        t_b[po:po + 64, :], op=OP.add)
                S.emit("vector", f, reads=["t_b", "xs%d" % hc],
                       writes=["xs%d" % hc], tag="yadd h%d w%d" % (h, w))
            return emit_y

        emit_qlat_diag(0, first=True)
        emit_negb(0)
        pending_negb = [None]
        prev_y = [None]
        for h in range(NH):
            for w in range(NW):
                hook = None
                if w == 0 and pending_negb[0] is not None:
                    hook = pending_negb[0]
                    pending_negb[0] = None
                prev_y[0] = emit_window(h, w, negb_hook=hook, prev_y=prev_y[0])
            if h + 1 < NH:
                emit_qlat_diag(h + 1)
                pending_negb[0] = (lambda hh: (lambda: emit_negb(hh)))(h + 1)
        prev_y[0]()

        def ln_stats_mm(phase, w, sps, qps, skey, qkey):
            wsl = slice(512 * w, 512 * w + 512)
            for cc_ in range(CC):
                S.emit("scalar", lambda e, cc_=cc_, wsl=wsl: e.copy(
                    xbln[:, cc_ % 2, :], xs[:, cc_, wsl]),
                    reads=["xs%d" % cc_], writes=["xbln%d" % (cc_ % 2)],
                    tag="%s cast" % phase)
                S.emit("scalar", lambda e, cc_=cc_, wsl=wsl: e.activation(
                    out=tst[:, cc_ % 2, :], in_=xs[:, cc_, wsl], func=AF.Square),
                    reads=["xs%d" % cc_], writes=["tst%d" % (cc_ % 2)],
                    tag="%s sq" % phase)
                def f(e, cc_=cc_, sps=sps):
                    return e.matmul(sps, onesm[:, :], xbln[:, cc_ % 2, :],
                                    start=(cc_ == 0), stop=(cc_ == CC - 1))
                S.emit("tensor", f, reads=["onesm", "xbln%d" % (cc_ % 2)],
                       writes=[skey], tag="%s smm" % phase)
                def f(e, cc_=cc_, qps=qps):
                    return e.matmul(qps, onesm[:, :], tst[:, cc_ % 2, :],
                                    start=(cc_ == 0), stop=(cc_ == CC - 1))
                S.emit("tensor", f, reads=["onesm", "tst%d" % (cc_ % 2)],
                       writes=[qkey], tag="%s qmm" % phase)

        def ln_evac(phase, sps, qps, skey, qkey):
            S.emit("vector", lambda e, sps=sps: e.tensor_scalar(
                out=mtile[:, :], in0=sps, scalar1=1.0 / C, scalar2=None,
                op0=OP.mult),
                reads=[skey], writes=["mtile"], tag="%s m" % phase)
            S.emit("vector", lambda e, qps=qps: e.tensor_scalar(
                out=t_a[:, :], in0=qps, scalar1=1.0 / C, scalar2=EPS,
                op0=OP.mult, op1=OP.add),
                reads=[qkey], writes=["t_a"], tag="%s q" % phase)

        def ln_chain(phase, w):
            S.emit("scalar", lambda e: e.activation(
                out=t_b[:, :], in_=mtile[:, :], func=AF.Square),
                reads=["mtile"], writes=["t_b"])
            S.emit("gpsimd", lambda e: e.tensor_tensor(
                t_a[:, :], t_a[:, :], t_b[:, :], op=OP.subtract),
                reads=["t_a", "t_b"], writes=["t_a"], tag="%s ve" % phase)
            S.emit("scalar", lambda e: e.activation(
                out=t_b[:, :], in_=t_a[:, :], func=AF.Ln),
                reads=["t_a"], writes=["t_b"], tag="%s ln" % phase)
            S.emit("scalar", lambda e: e.activation(
                out=rtile[:, :], in_=t_b[:, :], func=AF.Exp, scale=-0.5),
                reads=["t_b"], writes=["rtile"], tag="%s rsqrt" % phase)

        def ln_norm(phase, w):
            wsl = slice(512 * w, 512 * w + 512)
            for cc_ in range(CC):
                def f(e, cc_=cc_, wsl=wsl):
                    return e.tensor_tensor(
                        xs[:, cc_, wsl], xs[:, cc_, wsl], mtile[:, :], op=OP.subtract)
                S.emit("gpsimd", f, reads=["xs%d" % cc_, "mtile"],
                       writes=["xs%d" % cc_], tag="%s sub" % phase)
                def f(e, cc_=cc_, wsl=wsl):
                    return e.tensor_tensor(
                        xs[:, cc_, wsl], xs[:, cc_, wsl], rtile[:, :], op=OP.mult)
                S.emit("vector", f, reads=["xs%d" % cc_, "rtile"],
                       writes=["xs%d" % cc_], tag="%s mul" % phase)

        def ffn_quant(w):
            wsl = slice(512 * w, 512 * w + 512)
            for cc_ in range(CC):
                S.emit("scalar", lambda e, cc_=cc_, wsl=wsl: e.copy(
                    xbln[:, cc_ % 2, :], xs[:, cc_, wsl]),
                    reads=["xs%d" % cc_], writes=["xbln%d" % (cc_ % 2)], tag="ffn cast")
                S.emit("vector", lambda e, cc_=cc_: e.tensor_scalar(
                    out=xq_st[:, cc_, :], in0=xbln[:, cc_ % 2, :], scalar1=rinb[:, :],
                    scalar2=None, op0=OP.mult),
                    reads=["xbln%d" % (cc_ % 2), "rinb"], writes=["xq%d" % cc_],
                    tag="ffn quant")
            xqkeys = ["xq%d" % k for k in range(CC)]
            S.emit("sync", lambda e: e.dma_start(
                out=xq_st[:, :, :].bitcast(I8), in_=xq_st[:, :, :].bitcast(I8)),
                reads=xqkeys, writes=xqkeys, is_dma=True, tag="xqsync")

        def ffn_w(w):
            wsl = slice(512 * w, 512 * w + 512)
            for m in range(CC):
                buf, bkey = (ping, "ping") if m % 2 == 0 else (pong, "pong")
                def f(e, m=m, buf=buf):
                    for k in range(CC):
                        inst = e.matmul(buf[:, 0:512], kq8[:, k, 128 * m:128 * m + 128],
                                        xq_st[:, k, :],
                                        start=(k == 0), stop=(k == CC - 1))
                    return inst
                S.emit("tensor", f, reads=["kq8"] + ["xq%d" % k for k in range(CC)],
                       writes=[bkey], tag="ffn mm m%d w%d" % (m, w))
                S.emit("scalar", lambda e, m=m, buf=buf: e.activation(
                    out=tst[:, m % 2, :], in_=buf[:, 0:512], func=AF.Identity,
                    scale=sscale[:, :], bias=biasf[:, m:m + 1]),
                    reads=[bkey, "sscale", "biasf"], writes=["tst%d" % (m % 2)],
                    tag="ffn ev")
                def f(e, m=m, wsl=wsl):
                    return e.tensor_tensor(
                        xs[:, m, wsl], xs[:, m, wsl], tst[:, m % 2, :], op=OP.add)
                S.emit("gpsimd", f,
                       reads=["tst%d" % (m % 2), "xs%d" % m],
                       writes=["xs%d" % m], tag="ffn add")

        LN1B = (rsps[:, 0:512], auxps[:, 0:512], "rsps", "auxps")
        LN2B = (ctxps[:, 0, 0:512], ctxps[:, 1, 0:512], "ctx0", "ctx1")
        ln_stats_mm("ln1", 0, *LN1B)
        for w in range(NW + 1):
            if w < NW:
                ln_evac("ln1", *LN1B)
                if w + 1 < NW:
                    ln_stats_mm("ln1", w + 1, *LN1B)
            if w >= 1:
                ffn_w(w - 1)
                ln_stats_mm("ln2", w - 1, *LN2B)
            if w < NW:
                ln_chain("ln1", w)
                ln_norm("ln1", w)
                ffn_quant(w)
            if w >= 1:
                ln_evac("ln2", *LN2B)
                ln_chain("ln2", w - 1)
                ln_norm("ln2", w - 1)
                for cc_ in range(CC):
                    def f(e, cc_=cc_, w=w):
                        return e.dma_start(
                            out=d_out[128 * cc_:128 * cc_ + 128,
                                      512 * (w - 1):512 * w],
                            in_=xs[:, cc_, 512 * (w - 1):512 * w])
                    S.emit("sync", f, reads=["xs%d" % cc_], is_dma=True,
                           tag="st w%d c%d" % (w - 1, cc_))

        with nc.Block() as block:
            S.finalize(nc, block, sems)
    return nc


_NC_CACHE = {}

def _get_program(T):
    if T not in _NC_CACHE:
        _NC_CACHE[T] = build_program(T)
    return _NC_CACHE[T]


def make_in_map(xT, W_dq, W_uq, W_dkv, W_uk, W_uv, W_o, Wk, s_in, s_k, fbias):
    bf = ml_dtypes.bfloat16
    eye = np.eye(128, dtype=bf)
    eyef = np.eye(128, dtype=np.float32)
    mask = np.where(np.arange(128)[None, :] <= np.arange(128)[:, None],
                    np.float32(0.0), np.float32(NEG)).astype(bf)
    onesm_h = np.ones((128, 128), dtype=bf)
    onesr_h = np.ones((1, 128), dtype=np.float32)
    return {
        "xT": np.ascontiguousarray(xT, dtype=np.float32),
        "W_dq": np.asarray(W_dq, dtype=np.float32),
        "W_uq": np.asarray(W_uq, dtype=np.float32),
        "W_dkv": np.asarray(W_dkv, dtype=np.float32),
        "W_uk": np.asarray(W_uk, dtype=np.float32),
        "W_uv": np.asarray(W_uv, dtype=np.float32),
        "W_o": np.asarray(W_o, dtype=np.float32),
        "Wk": np.asarray(Wk).astype(bf),
        "s_in": np.ascontiguousarray(np.broadcast_to(
            np.asarray(s_in, dtype=np.float32).reshape(1, 1), (128, 1))),
        "s_k": np.ascontiguousarray(np.broadcast_to(
            np.asarray(s_k, dtype=np.float32).reshape(1, 1), (128, 1))),
        "fbias": np.ascontiguousarray(
            np.asarray(fbias).astype(bf).reshape(-1, 128).T),
        "eye128": eye, "eyef128": eyef, "maskqk": mask,
        "onesm": onesm_h, "onesrow": onesr_h,
    }


def kernel(L0_attn_x, L0_attn_W_dq, L0_attn_W_uq, L0_attn_W_dkv, L0_attn_W_uk,
           L0_attn_W_uv, L0_attn_W_o, L0_ffn_kernel, L0_ffn_input_scale,
           L0_ffn_kernel_scale, L0_ffn_bias):
    x = np.asarray(L0_attn_x, dtype=np.float32)
    B, T, C_ = x.shape
    nc = _get_program(T)
    in_maps = []
    for b in range(B):
        in_maps.append(make_in_map(
            x[b].T, L0_attn_W_dq, L0_attn_W_uq, L0_attn_W_dkv, L0_attn_W_uk,
            L0_attn_W_uv, L0_attn_W_o, L0_ffn_kernel, L0_ffn_input_scale,
            L0_ffn_kernel_scale, L0_ffn_bias))
    res = run_bass_kernel_spmd(nc, in_maps, core_ids=list(range(B)))
    out = np.stack([np.ascontiguousarray(r["out"].T) for r in res.results], axis=0)
    return out.astype(np.float32)


# version 32 (final)
# speedup vs baseline: 1.0234x; 1.0234x over previous
import math
import numpy as np
import ml_dtypes
import concourse.bass as bass
import concourse.bass_isa as bass_isa
import concourse.mybir as mybir
from concourse.bass_utils import run_bass_kernel_spmd

F32 = mybir.dt.float32
BF16 = mybir.dt.bfloat16
FP8 = mybir.dt.float8e4
I8 = mybir.dt.int8
I16 = mybir.dt.int16
AX = mybir.AxisListType.X
AF = mybir.ActivationFunctionType
OP = mybir.AluOpType

C = 1024
NH = 16
HS = 64
KVL = 256
QL = 256
NEG = -1.0e9
EPS = 1e-5
INV_SQRT_HS = 0.125

class Op:
    __slots__ = ("i", "eng", "fn", "deps", "is_dma", "inc", "tag")
    def __init__(self, i, eng, fn, deps, is_dma, tag):
        self.i, self.eng, self.fn, self.deps = i, eng, fn, deps
        self.is_dma, self.inc, self.tag = is_dma, False, tag

class Sched:
    ENGINES = ("tensor", "vector", "scalar", "gpsimd", "sync")

    def __init__(self):
        self.ops = []
        self.writer = {}
        self.readers = {}

    def emit(self, eng, fn, reads=(), writes=(), extra=(), is_dma=False, tag=""):
        deps = set(d.i for d in extra if d is not None)
        for k in reads:
            w = self.writer.get(k)
            if w is not None:
                deps.add(w)
        for k in writes:
            w = self.writer.get(k)
            if w is not None:
                deps.add(w)
            for r in self.readers.get(k, ()):
                deps.add(r)
        op = Op(len(self.ops), eng, fn, sorted(deps), is_dma, tag)
        self.ops.append(op)
        for k in reads:
            self.readers.setdefault(k, []).append(op.i)
        for k in writes:
            self.writer[k] = op.i
            self.readers[k] = []
        return op

    def finalize(self, nc, block, sems):
        ops = self.ops
        for op in ops:
            for d in op.deps:
                dep = ops[d]
                if dep.eng != op.eng or dep.is_dma or dep.eng != "tensor":
                    dep.inc = True
        for op in ops:
            if op.is_dma:
                op.inc = True
        comp_count = {}
        dma_count = {}
        run_comp = {e: 0 for e in self.ENGINES}
        run_dma = {e: 0 for e in self.ENGINES}
        for op in ops:
            if op.is_dma:
                run_dma[op.eng] += 1
                dma_count[op.i] = run_dma[op.eng]
            elif op.inc:
                run_comp[op.eng] += 1
                comp_count[op.i] = run_comp[op.eng]
        n = len(ops)
        prefix = {e: [0] * (n + 1) for e in self.ENGINES}
        for e in self.ENGINES:
            p = prefix[e]
            for i, op in enumerate(ops):
                p[i + 1] = p[i] + (1 if (op.is_dma and op.eng == e) else 0)
        waits = {i: {} for i in range(n)}
        for op in ops:
            for d in op.deps:
                dep = ops[d]
                if dep.is_dma:
                    sem = "dq_" + dep.eng
                    val = 16 * prefix[dep.eng][op.i]
                    assert val >= 16 * dma_count[dep.i]
                elif dep.eng == op.eng and dep.eng == "tensor":
                    continue
                else:
                    sem = "c_" + dep.eng
                    val = comp_count[dep.i]
                w = waits[op.i]
                if w.get(sem, -1) < val:
                    w[sem] = val
        for e in self.ENGINES:
            bounds = sorted({v for i_, w in waits.items() for s_, v in w.items()
                             if s_ == "dq_" + e})
            if not bounds:
                continue
            k = 0
            for op in ops:
                if not (op.is_dma and op.eng == e):
                    continue
                gate = 0
                for b in bounds:
                    if b <= 16 * k:
                        gate = b
                    else:
                        break
                if gate:
                    w = waits[op.i]
                    sem = "dq_" + e
                    if w.get(sem, -1) < gate:
                        w[sem] = gate
                k += 1
        per_eng = {e: [op for op in ops if op.eng == e] for e in self.ENGINES}

        def emit_engine(eng_name, eng):
            waited = {}
            for op in per_eng[eng_name]:
                for sem_name, val in sorted(waits[op.i].items()):
                    if waited.get(sem_name, -1) >= val:
                        continue
                    eng.wait_ge(sems[sem_name], val)
                    waited[sem_name] = val
                inst = op.fn(eng)
                if op.inc:
                    if op.is_dma:
                        inst.then_inc(sems["dq_" + op.eng], 16)
                    else:
                        assert inst is not None, f"op {op.tag} returned no inst"
                        inst.then_inc(sems["c_" + op.eng], 1)
            if eng_name == "sync":
                eng.wait_ge(sems["dq_sync"], 16 * prefix["sync"][n])
            if eng_name == "gpsimd" and prefix["gpsimd"][n]:
                eng.wait_ge(sems["dq_gpsimd"], 16 * prefix["gpsimd"][n])
            if eng_name == "scalar" and prefix["scalar"][n]:
                eng.wait_ge(sems["dq_scalar"], 16 * prefix["scalar"][n])

        @block.tensor
        def _(e):
            emit_engine("tensor", e)

        @block.vector
        def _(e):
            emit_engine("vector", e)

        @block.scalar
        def _(e):
            emit_engine("scalar", e)

        @block.gpsimd
        def _(e):
            emit_engine("gpsimd", e)

        @block.sync
        def _(e):
            emit_engine("sync", e)


def build_program(T=2048):
    NT = T // 128
    NW = T // 512
    CC = C // 128
    POOLW = NT * 512

    nc = bass.Bass("TRN2", target_bir_lowering=False)
    d_x = nc.dram_tensor("xT", [C, T], F32, kind="ExternalInput")
    d_wdq = nc.dram_tensor("W_dq", [QL, C], F32, kind="ExternalInput")
    d_wuq = nc.dram_tensor("W_uq", [C, QL], F32, kind="ExternalInput")
    d_wdkv = nc.dram_tensor("W_dkv", [KVL, C], F32, kind="ExternalInput")
    d_wuk = nc.dram_tensor("W_uk", [C, KVL], F32, kind="ExternalInput")
    d_wuv = nc.dram_tensor("W_uv", [C, KVL], F32, kind="ExternalInput")
    d_wo = nc.dram_tensor("W_o", [C, C], F32, kind="ExternalInput")
    d_wk = nc.dram_tensor("Wk", [C, C], BF16, kind="ExternalInput")
    d_sin = nc.dram_tensor("s_in", [128, 1], F32, kind="ExternalInput")
    d_sk = nc.dram_tensor("s_k", [128, 1], F32, kind="ExternalInput")
    d_bias = nc.dram_tensor("fbias", [128, C // 128], BF16, kind="ExternalInput")
    d_eye = nc.dram_tensor("eye128", [128, 128], BF16, kind="ExternalInput")
    d_eyef = nc.dram_tensor("eyef128", [128, 128], F32, kind="ExternalInput")
    d_mask = nc.dram_tensor("maskqk", [128, 128], BF16, kind="ExternalInput")
    d_ones = nc.dram_tensor("onesm", [128, 128], BF16, kind="ExternalInput")
    d_onesr = nc.dram_tensor("onesrow", [1, 128], F32, kind="ExternalInput")
    d_out = nc.dram_tensor("out", [C, T], F32, kind="ExternalOutput")

    from contextlib import ExitStack
    with ExitStack() as stack:
        E_ = stack.enter_context
        xs = E_(nc.sbuf_tensor("xs", [128, CC, T], F32))
        eyeb = E_(nc.sbuf_tensor("eyeb", [128, 128], BF16))
        maskb = E_(nc.sbuf_tensor("maskb", [128, 128], BF16))
        onesm = E_(nc.sbuf_tensor("onesm_s", [128, 128], BF16))
        eyef = E_(nc.sbuf_tensor("eyef_s", [128, 128], F32))
        onesr = E_(nc.sbuf_tensor("onesr_s", [1, 128], F32))
        wdq_b = E_(nc.sbuf_tensor("wdq_b", [128, 2, C], BF16))
        wuq_b = E_(nc.sbuf_tensor("wuq_b", [128, CC, QL], BF16))
        wdkv_b = E_(nc.sbuf_tensor("wdkv_b", [128, 2, C], BF16))
        wuk_b = E_(nc.sbuf_tensor("wuk_b", [128, CC, KVL], BF16))
        wuv_b = E_(nc.sbuf_tensor("wuv_b", [128, CC, KVL], BF16))
        wdqT = E_(nc.sbuf_tensor("wdqT", [128, CC, QL], BF16))
        wdkvT = E_(nc.sbuf_tensor("wdkvT", [128, CC, KVL], BF16))
        keff = E_(nc.sbuf_tensor("keff", [128, CC, KVL], BF16))
        emat = E_(nc.sbuf_tensor("emat", [128, 2, NH * KVL], BF16))
        veff = E_(nc.sbuf_tensor("veff", [128, 2, C], BF16))
        kq8 = E_(nc.sbuf_tensor("kq8", [128, CC, C], FP8))
        wo_st = E_(nc.sbuf_tensor("wo_st", [128, 2, C], F32))
        wk_st = E_(nc.sbuf_tensor("wk_st", [128, 1, C], BF16))
        xb_st = E_(nc.sbuf_tensor("xb_st", [128, 2, 512], BF16))
        cq = E_(nc.sbuf_tensor("cq", [128, 2, T], BF16))
        ckvT = E_(nc.sbuf_tensor("ckvT", [128, 2, T], BF16))
        ckvkm = E_(nc.sbuf_tensor("ckvkm", [128, NT, 2, 128], BF16))
        qlat = E_(nc.sbuf_tensor("qlat", [128, 2, T], BF16))
        pool = E_(nc.sbuf_tensor("pool", [128, NT, 512], BF16))
        ctxs = E_(nc.sbuf_tensor("ctxs", [128, 2, 512], BF16))
        dmax = E_(nc.sbuf_tensor("dmax", [128, 16], F32))
        gmax = E_(nc.sbuf_tensor("gmax", [128, 1], F32))
        armax = E_(nc.sbuf_tensor("armax", [128, 1], F32))
        negB = E_(nc.sbuf_tensor("negB", [128, 2], F32))
        rbc = E_(nc.sbuf_tensor("rbc", [128, 512], F32))
        sc_st = E_(nc.sbuf_tensor("sc_st", [1, 4], F32))
        sinb = E_(nc.sbuf_tensor("sinb", [128, 1], F32))
        skb = E_(nc.sbuf_tensor("skb", [128, 1], F32))
        rinb = E_(nc.sbuf_tensor("rinb", [128, 1], F32))
        rkb = E_(nc.sbuf_tensor("rkb", [128, 1], F32))
        sscale = E_(nc.sbuf_tensor("sscale", [128, 1], F32))
        bias_st = E_(nc.sbuf_tensor("bias_st", [128, CC], BF16))
        biasf = E_(nc.sbuf_tensor("biasf", [128, CC], F32))
        nmax = E_(nc.sbuf_tensor("nmax", [1, 1], F32))
        mtile = E_(nc.sbuf_tensor("mtile", [128, 512], F32))
        rtile = E_(nc.sbuf_tensor("rtile", [128, 512], F32))
        t_a = E_(nc.sbuf_tensor("t_a", [128, 512], F32))
        t_b = E_(nc.sbuf_tensor("t_b", [128, 512], F32))
        xbln = E_(nc.sbuf_tensor("xbln", [128, 2, 512], BF16))
        xq_st = E_(nc.sbuf_tensor("xq_st", [128, CC, 512], FP8))
        tst = E_(nc.sbuf_tensor("tst", [128, 2, 512], BF16))

        ping = E_(nc.psum_tensor("ping", [128, 1024], F32))
        pong = E_(nc.psum_tensor("pong", [128, 1024], F32))
        ctxps = E_(nc.psum_tensor("ctxps", [128, 2, 512], F32))
        rsps = E_(nc.psum_tensor("rsps", [128, 512], F32))
        auxps = E_(nc.psum_tensor("auxps", [128, 512], F32))

        sems = {}
        for e in Sched.ENGINES:
            sems["c_" + e] = E_(nc.semaphore("c_" + e))
        sems["dq_sync"] = E_(nc.semaphore("dq_sync"))
        sems["dq_gpsimd"] = E_(nc.semaphore("dq_gpsimd"))
        sems["dq_scalar"] = E_(nc.semaphore("dq_scalar"))

        S = Sched()

        S.emit("sync", lambda e: e.dma_start(out=eyeb[:, :], in_=d_eye[:, :]),
               writes=["eyeb"], is_dma=True)
        S.emit("sync", lambda e: e.dma_start(out=maskb[:, :], in_=d_mask[:, :]),
               writes=["maskb"], is_dma=True)
        S.emit("sync", lambda e: e.dma_start(out=onesm[:, :], in_=d_ones[:, :]),
               writes=["onesm"], is_dma=True)
        S.emit("sync", lambda e: e.dma_start(out=eyef[:, :], in_=d_eyef[:, :]),
               writes=["eyef"], is_dma=True)
        S.emit("sync", lambda e: e.dma_start(out=onesr[:, :], in_=d_onesr[:, :]),
               writes=["onesr"], is_dma=True)
        S.emit("sync", lambda e: e.dma_start(out=bias_st[:, :], in_=d_bias[:, :]),
               writes=["bias_st"], is_dma=True)
        S.emit("sync", lambda e: e.dma_start(out=sinb[:, :], in_=d_sin[:, :]),
               writes=["sinb"], is_dma=True)
        S.emit("sync", lambda e: e.dma_start(out=skb[:, :], in_=d_sk[:, :]),
               writes=["skb"], is_dma=True)
        W5 = ((d_wdq, wdq_b, "wdq_b"), (d_wuq, wuq_b, "wuq_b"),
              (d_wdkv, wdkv_b, "wdkv_b"), (d_wuk, wuk_b, "wuk_b"),
              (d_wuv, wuv_b, "wuv_b"))
        if T >= 2048:
            for i, (dsrc, dst, dkey) in enumerate(W5):
                def f(e, dsrc=dsrc, i=i):
                    return e.dma_start(
                        out=xs[:, i, 0:2048].rearrange("p (a c) -> p a c", a=2048 // C),
                        in_=dsrc[:, :].rearrange("(a p) c -> p a c", p=128))
                S.emit("sync", f, writes=["xs%d" % i], is_dma=True, tag="ldw%d" % i)
                S.emit("vector", lambda e, dst=dst, i=i: e.tensor_copy(
                    dst[:, :, :], xs[:, i, 0:2048]),
                    reads=["xs%d" % i], writes=[dkey], tag="castw%d" % i)
        else:
            for i, (dsrc, dst, dkey) in enumerate(W5):
                def f(e, dsrc=dsrc, dst=dst):
                    return e.dma_start(
                        out=dst[:, :, :],
                        in_=dsrc[:, :].rearrange("(a p) c -> p a c", p=128))
                S.emit("gpsimd", f, writes=[dkey], is_dma=True, tag="ldw%d" % i)

        def emit_kq8():
            for cr in range(CC):
                S.emit("sync", lambda e, cr=cr: e.dma_start(
                    out=wk_st[:, 0, :], in_=d_wk[128 * cr:128 * cr + 128, :]),
                    writes=["wk"], is_dma=True, tag="ld_wk%d" % cr)
                S.emit("vector", lambda e, cr=cr: e.tensor_scalar(
                    out=kq8[:, cr, :], in0=wk_st[:, 0, :], scalar1=rkb[:, :],
                    scalar2=None, op0=OP.mult),
                    reads=["wk", "rkb"], writes=["kq8"], tag="kq8")
            S.emit("scalar", lambda e: e.dma_start(
                out=kq8[:, :, :].bitcast(I8), in_=kq8[:, :, :].bitcast(I8)),
                reads=["kq8"], writes=["kq8"], is_dma=True, tag="kq8sync")


        S.emit("vector", lambda e: e.reciprocal(rinb[:, :], sinb[:, :]),
               reads=["sinb"], writes=["rinb"])
        S.emit("vector", lambda e: e.reciprocal(rkb[:, :], skb[:, :]),
               reads=["skb"], writes=["rkb"])
        S.emit("vector", lambda e: e.tensor_tensor(sscale[:, :], sinb[:, :], skb[:, :], op=OP.mult),
               reads=["sinb", "skb"], writes=["sscale"])
        S.emit("vector", lambda e: e.tensor_copy(biasf[:, :], bias_st[:, :]),
               reads=["bias_st"], writes=["biasf"])
        emit_kq8()

        def gen_u_keff():
            for m in range(2):
                def f(e, m=m):
                    for cc_ in range(CC):
                        inst = e.matmul(ctxps[:, m, 0:KVL],
                                        wuq_b[:, cc_, 128 * m:128 * m + 128],
                                        wuk_b[:, cc_, :],
                                        start=(cc_ == 0), stop=(cc_ == CC - 1))
                    return inst
                S.emit("tensor", f, reads=["wuq_b", "wuk_b"], writes=["ctx%d" % m], tag="U")
                S.emit("vector", lambda e, m=m: e.tensor_copy(ctxs[:, m, 0:KVL], ctxps[:, m, 0:KVL]),
                       reads=["ctx%d" % m], writes=["ctxs%d" % m])
                yield
            for mm_ in range(CC):
                def f(e, mm_=mm_):
                    for qlc in range(2):
                        inst = e.matmul(ctxps[:, mm_ % 2, 0:KVL],
                                        wdq_b[:, qlc, 128 * mm_:128 * mm_ + 128],
                                        ctxs[:, qlc, 0:KVL],
                                        start=(qlc == 0), stop=(qlc == 1))
                    return inst
                S.emit("tensor", f, reads=["wdq_b", "ctxs0", "ctxs1"], writes=["ctx%d" % (mm_ % 2)], tag="keff")
                S.emit("vector", lambda e, mm_=mm_: e.tensor_copy(
                    keff[:, mm_, :], ctxps[:, mm_ % 2, 0:KVL]),
                    reads=["ctx%d" % (mm_ % 2)], writes=["keff"])
                yield

        def gen_emat():
            for _ in range(12):
                yield
            for h in range(NH):
                po = (h % 2) * 64
                hc = h // 2
                for m in range(2):
                    def f(e, po=po, hc=hc, m=m):
                        return e.matmul(auxps[:, 0:KVL],
                                        wuq_b[po:po + 64, hc, 128 * m:128 * m + 128],
                                        keff[po:po + 64, hc, :],
                                        start=True, stop=True)
                    S.emit("tensor", f, reads=["wuq_b", "keff"], writes=["auxps"], tag="E")
                    S.emit("vector", lambda e, h=h, m=m: e.tensor_copy(
                        emat[:, m, KVL * h:KVL * (h + 1)], auxps[:, 0:KVL]),
                        reads=["auxps"], writes=["emat"])
                    yield

        def gen_wt(stage_idx, bsrc, bskey, dst, dkey, stage, stkey):
            for qlc in range(2):
                for cc_ in range(CC):
                    if T >= 2048:
                        def f(e, stage_idx=stage_idx, qlc=qlc, cc_=cc_, stage=stage):
                            return e.transpose(
                                stage[:, 0:128],
                                xs[:, stage_idx, 1024 * qlc + 128 * cc_:1024 * qlc + 128 * cc_ + 128],
                                eyef[:, :])
                        S.emit("tensor", f, reads=["xs%d" % stage_idx, "eyef"],
                               writes=[stkey], tag="wT")
                        S.emit("vector", lambda e, dst=dst, cc_=cc_, qlc=qlc, stage=stage: e.tensor_copy(
                            dst[:, cc_, 128 * qlc:128 * qlc + 128], stage[:, 0:128]),
                            reads=[stkey], writes=[dkey])
                    else:
                        def f(e, bsrc=bsrc, qlc=qlc, cc_=cc_, stage=stage):
                            return e.transpose(stage[:, 0:64].bitcast(BF16),
                                               bsrc[:, qlc, 128 * cc_:128 * cc_ + 128],
                                               eyeb[:, :])
                        S.emit("tensor", f, reads=[bskey, "eyeb"], writes=[stkey], tag="wT")
                        S.emit("vector", lambda e, dst=dst, cc_=cc_, qlc=qlc, stage=stage: e.tensor_copy(
                            dst[:, cc_, 128 * qlc:128 * qlc + 128], stage[:, 0:64].bitcast(BF16)),
                            reads=[stkey], writes=[dkey])
                    yield

        def gen_veff():
            def ld(cr):
                S.emit("gpsimd", lambda e, cr=cr: e.dma_start(
                    out=wo_st[:, cr % 2, :], in_=d_wo[128 * cr:128 * cr + 128, :]),
                    writes=["wo%d" % (cr % 2)], is_dma=True, tag="ld_wo%d" % cr)
            ld(0)
            for cr in range(CC):
                if cr + 1 < CC:
                    ld(cr + 1)
                for jc in range(CC):
                    def f(e, cr=cr, jc=jc):
                        return e.transpose(pong[:, 0:128],
                                           wo_st[:, cr % 2, 128 * jc:128 * jc + 128],
                                           eyef[:, :])
                    S.emit("tensor", f, reads=["wo%d" % (cr % 2), "eyef"],
                           writes=["pong"], tag="woT")
                    S.emit("vector", lambda e, jc=jc: e.tensor_copy(
                        pool[:, jc // 4, 128 * (jc % 4):128 * (jc % 4) + 128],
                        pong[:, 0:128]),
                        reads=["pong"], writes=["pw%d" % (jc // 4)])
                    if jc % 4 == 3:
                        yield
                for m in range(2):
                    def f(e, m=m):
                        for jc in range(CC):
                            inst = e.matmul(pong[:, 512 + 128 * m:512 + 128 * m + 128],
                                            wuv_b[:, jc, 128 * m:128 * m + 128],
                                            pool[:, jc // 4, 128 * (jc % 4):128 * (jc % 4) + 128],
                                            start=(jc == 0), stop=(jc == CC - 1))
                        return inst
                    S.emit("tensor", f, reads=["wuv_b", "pw0", "pw1"],
                           writes=["pong"], tag="veff")
                    S.emit("vector", lambda e, m=m, cr=cr: e.tensor_copy(
                        veff[:, m, 128 * cr:128 * cr + 128],
                        pong[:, 512 + 128 * m:512 + 128 * m + 128]),
                        reads=["pong"], writes=["veff"])
                    yield

        gens = [gen_u_keff(), gen_emat(),
                gen_wt(0, wdq_b, "wdq_b", wdqT, "wdqT", rsps, "rsps"),
                gen_wt(2, wdkv_b, "wdkv_b", wdkvT, "wdkvT", ping, "ping"),
                gen_veff()]
        while gens:
            alive = []
            for g in gens:
                try:
                    next(g)
                    alive.append(g)
                except StopIteration:
                    continue
            gens = alive

        S.emit("sync", lambda e: e.dma_start(
            out=xs[:, :, :], in_=d_x[:, :].rearrange("(c p) t -> p c t", p=128)),
            writes=["xs%d" % c for c in range(CC)], is_dma=True, tag="ld_x")
        for w in range(NW):
            sl = slice(512 * w, 512 * w + 512)
            for cc_ in range(CC):
                S.emit("vector", lambda e, cc_=cc_, sl=sl: e.tensor_copy(
                    xb_st[:, cc_ % 2, :], xs[:, cc_, sl]),
                    reads=["xs%d" % cc_], writes=["xb%d" % (cc_ % 2)], tag="xbcast")
                for qlc in range(2):
                    def f(e, cc_=cc_, qlc=qlc):
                        return e.matmul(ping[:, 512 * qlc:512 * qlc + 512],
                                        wdqT[:, cc_, 128 * qlc:128 * qlc + 128],
                                        xb_st[:, cc_ % 2, :],
                                        start=(cc_ == 0), stop=(cc_ == CC - 1))
                    S.emit("tensor", f, reads=["wdqT", "xb%d" % (cc_ % 2)],
                           writes=["ping"], tag="cq")
                for lc in range(2):
                    def f(e, cc_=cc_, lc=lc):
                        return e.matmul(pong[:, 512 * lc:512 * lc + 512],
                                        wdkvT[:, cc_, 128 * lc:128 * lc + 128],
                                        xb_st[:, cc_ % 2, :],
                                        start=(cc_ == 0), stop=(cc_ == CC - 1))
                    S.emit("tensor", f, reads=["wdkvT", "xb%d" % (cc_ % 2)],
                           writes=["pong"], tag="ckv")
            for qlc in range(2):
                S.emit("vector", lambda e, qlc=qlc, sl=sl: e.tensor_copy(
                    cq[:, qlc, sl], ping[:, 512 * qlc:512 * qlc + 512]),
                    reads=["ping"], writes=["cq"])
            for lc in range(2):
                S.emit("vector", lambda e, lc=lc, sl=sl: e.tensor_copy(
                    ckvT[:, lc, sl], pong[:, 512 * lc:512 * lc + 512]),
                    reads=["pong"], writes=["ckvT"])

        all_dma = [op for op in S.ops if op.is_dma]
        for lc in range(2):
            def f(e, lc=lc):
                return e.dma_start_transpose(out=ckvkm[:, :, lc, :], in_=ckvT[:, lc, 0:T])
            S.emit("sync", f, reads=["ckvT"], writes=["ckvkm"],
                   extra=all_dma, is_dma=True, tag="ckvkm_T")

        rot = [0]

        def score_slot():
            i = rot[0] % 4
            rot[0] += 1
            buf = ping if i < 2 else pong
            off = 512 * (i % 2)
            return buf, off, ("ping" if i < 2 else "pong")

        def emit_qlat_diag(h, first=False):
            nhalf = NT // 8 + (1 if NT % 8 else 0)
            for w in range(NW):
                sl = slice(512 * w, 512 * w + 512)
                for lc in range(2):
                    buf, off, bkey = score_slot()
                    def f(e, h=h, lc=lc, sl=sl, buf=buf, off=off):
                        for qlc in range(2):
                            inst = e.matmul(
                                buf[:, off:off + 512],
                                emat[:, qlc, KVL * h + 128 * lc:KVL * h + 128 * lc + 128],
                                cq[:, qlc, sl],
                                start=(qlc == 0), stop=(qlc == 1))
                        return inst
                    S.emit("tensor", f, reads=["emat", "cq"], writes=[bkey],
                           tag="qlat h%d" % h)
                    S.emit("vector", lambda e, lc=lc, sl=sl, buf=buf, off=off: e.tensor_copy(
                        qlat[:, lc, sl], buf[:, off:off + 512]),
                        reads=[bkey], writes=["qlat"], tag="qlat_ev")
                dbuf, dkey = (ping, "ping") if (w // 2) % 2 == 0 else (pong, "pong")
                doff = 512 * (w % 2)
                def f(e, h=h, w=w, dbuf=dbuf, doff=doff):
                    for t_ in range(4):
                        kt = 4 * w + t_
                        for lc in range(2):
                            e.matmul(dbuf[:, doff + 128 * t_:doff + 128 * t_ + 128],
                                     ckvT[:, lc, 128 * kt:128 * kt + 128],
                                     qlat[:, lc, 128 * kt:128 * kt + 128],
                                     start=(lc == 0), stop=False)
                        inst = e.matmul(dbuf[:, doff + 128 * t_:doff + 128 * t_ + 128],
                                        maskb[:, :], eyeb[:, :],
                                        start=False, stop=True)
                    return inst
                S.emit("tensor", f, reads=["ckvT", "qlat", "maskb", "eyeb"],
                       writes=[dkey], tag="diag h%d w%d" % (h, w))
                def f(e, w=w, dbuf=dbuf, doff=doff):
                    return e.tensor_reduce(
                        dmax[:, 4 * w:4 * w + 4],
                        dbuf[:, doff:doff + 512].rearrange("p (t k) -> p t k", k=128),
                        axis=AX, op=OP.max)
                S.emit("vector", f, reads=[dkey], writes=["dmax"], tag="dmaxr")
            S.emit("vector", lambda e: e.tensor_reduce(
                gmax[:, :], dmax[:, 0:NT], axis=AX, op=OP.max),
                reads=["dmax"], writes=["gmax"])

        def emit_negb(h):
            hp = h % 2
            S.emit("tensor", lambda e: e.transpose(
                ctxps[0:1, 1, 0:128], gmax[:, :], eyef[:, :]),
                reads=["gmax", "eyef"], writes=["ctx1"], tag="gmaxT")
            S.emit("vector", lambda e: e.tensor_reduce(
                nmax[:, :], ctxps[0:1, 1, 0:128], axis=AX, op=OP.max, negate=True),
                reads=["ctx1"], writes=["nmax"], tag="nmaxr")
            S.emit("tensor", lambda e: e.matmul(
                ctxps[:, 0, 0:1], onesr[:, :], nmax[:, :], start=True, stop=True),
                reads=["onesr", "nmax"], writes=["ctx0"], tag="negBmm")
            S.emit("vector", lambda e, hp=hp: e.tensor_copy(
                negB[:, hp:hp + 1], ctxps[:, 0, 0:1]),
                reads=["ctx0"], writes=["negB%d" % hp], tag="negB")

        def emit_window(h, w, negb_hook=None, prev_y=None):
            hp = h % 2
            wsl = slice(512 * w, 512 * w + 512)
            kcs = list(range(min(4 * w + 4, NT)))
            kc_full = [kc for kc in kcs if 128 * kc <= 512 * w]
            kc_part = [kc for kc in kcs if kc not in kc_full]
            groups = [kc_full[i:i + 2] for i in range(0, len(kc_full), 2)]
            groups += [[kc] for kc in kc_part]

            def emit_group(grp, pre_exp=None):
                if len(grp) == 2 and rot[0] % 2 == 1:
                    rot[0] += 1
                slots = []
                for kc in grp:
                    q0 = max(512 * w, 128 * kc)
                    n = 512 * (w + 1) - q0
                    buf, off, bkey = score_slot()
                    slots.append((kc, q0, n, buf, off, bkey))
                    has_diag = (q0 == 128 * kc)
                    def f(e, kc=kc, q0=q0, n=n, buf=buf, off=off, has_diag=has_diag):
                        for lc in range(2):
                            inst = e.matmul(buf[:, off:off + n],
                                            ckvT[:, lc, 128 * kc:128 * kc + 128],
                                            qlat[:, lc, q0:q0 + n],
                                            start=(lc == 0),
                                            stop=(lc == 1 and not has_diag))
                        if has_diag:
                            inst = e.matmul(buf[:, off:off + 128],
                                            maskb[:, :], eyeb[:, :],
                                            start=False, stop=True)
                        return inst
                    S.emit("tensor", f, reads=["ckvT", "qlat", "maskb", "eyeb"],
                           writes=[bkey], tag="sT h%d w%d kc%d" % (h, w, kc))
                if pre_exp is not None:
                    pre_exp()
                if (len(slots) == 2 and slots[0][3] is slots[1][3]
                        and slots[0][4] + 512 == slots[1][4]
                        and slots[0][2] == 512 and slots[1][2] == 512
                        and slots[1][0] == slots[0][0] + 1):
                    kc, q0, n, buf, off, bkey = slots[0]
                    def f(e, kc=kc, buf=buf, off=off, hp=hp):
                        return e.activation(
                            out=pool[:, kc:kc + 2, :].rearrange("p a b -> p (a b)"),
                            in_=buf[:, off:off + 1024],
                            func=AF.Exp, bias=negB[:, hp:hp + 1], scale=INV_SQRT_HS)
                    S.emit("scalar", f,
                           reads=[bkey, slots[1][5], "negB%d" % hp],
                           writes=["pw%d" % kc, "pw%d" % (kc + 1)],
                           tag="exp2 h%d w%d kc%d" % (h, w, kc))
                else:
                    for (kc, q0, n, buf, off, bkey) in slots:
                        def f(e, kc=kc, n=n, buf=buf, off=off, hp=hp):
                            return e.activation(
                                out=pool[:, kc, 0:n], in_=buf[:, off:off + n],
                                func=AF.Exp, bias=negB[:, hp:hp + 1], scale=INV_SQRT_HS)
                        S.emit("scalar", f, reads=[bkey, "negB%d" % hp],
                               writes=["pw%d" % kc],
                               tag="exp h%d w%d kc%d" % (h, w, kc))

            def emit_consume(kc, first, last):
                q0 = max(512 * w, 128 * kc)
                n = 512 * (w + 1) - q0
                co = q0 - 512 * w
                def f(e, kc=kc, n=n, co=co, first=first, last=last):
                    return e.matmul(rsps[:, co:co + n], onesm[:, :],
                                    pool[:, kc, 0:n], start=first, stop=last)
                S.emit("tensor", f, reads=["onesm", "pw%d" % kc],
                       writes=["rsps"], tag="rsum h%d w%d kc%d" % (h, w, kc))
                for lc in range(2):
                    def f(e, kc=kc, n=n, co=co, lc=lc, first=first, last=last):
                        return e.matmul(ctxps[:, lc, co:co + n],
                                        ckvkm[:, kc, lc, :],
                                        pool[:, kc, 0:n], start=first, stop=last)
                    S.emit("tensor", f, reads=["ckvkm", "pw%d" % kc],
                           writes=["ctx%d" % lc], tag="ctx h%d w%d kc%d" % (h, w, kc))

            def first_hook():
                if negb_hook is not None:
                    negb_hook()
                if prev_y is not None:
                    prev_y()
            for j, grp in enumerate(groups):
                emit_group(grp, pre_exp=(first_hook if j == 0 else None))
                if j >= 2:
                    for kc in groups[j - 2]:
                        emit_consume(kc, first=(kc == kcs[0]), last=(kc == kcs[-1]))
            for grp in groups[-2:] if len(groups) >= 2 else groups:
                for kc in grp:
                    emit_consume(kc, first=(kc == kcs[0]), last=(kc == kcs[-1]))
            for lc in range(2):
                S.emit("scalar", lambda e, lc=lc: e.copy(
                    ctxs[:, lc, :], ctxps[:, lc, :]),
                    reads=["ctx%d" % lc], writes=["ctxs%d" % lc],
                    tag="ctxev")
            po = (h % 2) * 64
            hc = h // 2
            def emit_y():
                S.emit("vector", lambda e: e.reciprocal(rbc[:, :], rsps[:, 0:512]),
                       reads=["rsps"], writes=["rbc"], tag="recip")
                def f(e, h=h, po=po):
                    for lc in range(2):
                        inst = e.matmul(auxps[po:po + 64, 0:512],
                                        veff[:, lc, 64 * h:64 * h + 64],
                                        ctxs[:, lc, :],
                                        start=(lc == 0), stop=(lc == 1),
                                        tile_position=(0, po) if po else None)
                    return inst
                S.emit("tensor", f, reads=["veff", "ctxs0", "ctxs1"],
                       writes=["auxps"], tag="y h%d w%d" % (h, w))
                def f(e, po=po):
                    return e.tensor_tensor(
                        t_b[po:po + 64, :], auxps[po:po + 64, 0:512],
                        rbc[po:po + 64, :], op=OP.mult)
                S.emit("vector", f, reads=["auxps", "rbc"], writes=["t_b"],
                       tag="yscale h%d w%d" % (h, w))
                def f(e, po=po, hc=hc, wsl=wsl):
                    return e.tensor_tensor(
                        xs[po:po + 64, hc, wsl], xs[po:po + 64, hc, wsl],
                        t_b[po:po + 64, :], op=OP.add)
                S.emit("vector", f, reads=["t_b", "xs%d" % hc],
                       writes=["xs%d" % hc], tag="yadd h%d w%d" % (h, w))
            return emit_y

        emit_qlat_diag(0, first=True)
        emit_negb(0)
        pending_negb = [None]
        prev_y = [None]
        for h in range(NH):
            for w in range(NW):
                hook = None
                if w == 0 and pending_negb[0] is not None:
                    hook = pending_negb[0]
                    pending_negb[0] = None
                prev_y[0] = emit_window(h, w, negb_hook=hook, prev_y=prev_y[0])
            if h + 1 < NH:
                emit_qlat_diag(h + 1)
                pending_negb[0] = (lambda hh: (lambda: emit_negb(hh)))(h + 1)
        prev_y[0]()

        def ln_stats_mm(phase, w, sps, qps, skey, qkey):
            wsl = slice(512 * w, 512 * w + 512)
            for cc_ in range(CC):
                S.emit("scalar", lambda e, cc_=cc_, wsl=wsl: e.copy(
                    xbln[:, cc_ % 2, :], xs[:, cc_, wsl]),
                    reads=["xs%d" % cc_], writes=["xbln%d" % (cc_ % 2)],
                    tag="%s cast" % phase)
                S.emit("gpsimd" if cc_ % 2 else "vector",
                       lambda e, cc_=cc_, wsl=wsl: e.tensor_tensor(
                    tst[:, cc_ % 2, :], xs[:, cc_, wsl], xs[:, cc_, wsl], op=OP.mult),
                    reads=["xs%d" % cc_], writes=["tst%d" % (cc_ % 2)],
                    tag="%s sq" % phase)
                def f(e, cc_=cc_, sps=sps):
                    return e.matmul(sps, onesm[:, :], xbln[:, cc_ % 2, :],
                                    start=(cc_ == 0), stop=(cc_ == CC - 1))
                S.emit("tensor", f, reads=["onesm", "xbln%d" % (cc_ % 2)],
                       writes=[skey], tag="%s smm" % phase)
                def f(e, cc_=cc_, qps=qps):
                    return e.matmul(qps, onesm[:, :], tst[:, cc_ % 2, :],
                                    start=(cc_ == 0), stop=(cc_ == CC - 1))
                S.emit("tensor", f, reads=["onesm", "tst%d" % (cc_ % 2)],
                       writes=[qkey], tag="%s qmm" % phase)

        def ln_evac(phase, sps, qps, skey, qkey):
            S.emit("vector", lambda e, sps=sps: e.tensor_scalar(
                out=mtile[:, :], in0=sps, scalar1=1.0 / C, scalar2=None,
                op0=OP.mult),
                reads=[skey], writes=["mtile"], tag="%s m" % phase)
            S.emit("vector", lambda e, qps=qps: e.tensor_scalar(
                out=t_a[:, :], in0=qps, scalar1=1.0 / C, scalar2=EPS,
                op0=OP.mult, op1=OP.add),
                reads=[qkey], writes=["t_a"], tag="%s q" % phase)

        def ln_chain(phase, w):
            S.emit("vector", lambda e: e.tensor_tensor(
                t_b[:, :], mtile[:, :], mtile[:, :], op=OP.mult),
                reads=["mtile"], writes=["t_b"])
            S.emit("vector", lambda e: e.tensor_tensor(
                t_a[:, :], t_a[:, :], t_b[:, :], op=OP.subtract),
                reads=["t_a", "t_b"], writes=["t_a"], tag="%s ve" % phase)
            S.emit("scalar", lambda e: e.activation(
                out=t_b[:, :], in_=t_a[:, :], func=AF.Ln),
                reads=["t_a"], writes=["t_b"], tag="%s ln" % phase)
            S.emit("scalar", lambda e: e.activation(
                out=rtile[:, :], in_=t_b[:, :], func=AF.Exp, scale=-0.5),
                reads=["t_b"], writes=["rtile"], tag="%s rsqrt" % phase)

        def ln_norm(phase, w):
            wsl = slice(512 * w, 512 * w + 512)
            for cc_ in range(CC):
                eng = "gpsimd" if cc_ in (0, 3, 6) else "vector"
                def f(e, cc_=cc_, wsl=wsl):
                    return e.tensor_tensor(
                        xs[:, cc_, wsl], xs[:, cc_, wsl], mtile[:, :], op=OP.subtract)
                S.emit(eng, f, reads=["xs%d" % cc_, "mtile"],
                       writes=["xs%d" % cc_], tag="%s sub" % phase)
                def f(e, cc_=cc_, wsl=wsl):
                    return e.tensor_tensor(
                        xs[:, cc_, wsl], xs[:, cc_, wsl], rtile[:, :], op=OP.mult)
                S.emit(eng, f, reads=["xs%d" % cc_, "rtile"],
                       writes=["xs%d" % cc_], tag="%s mul" % phase)

        def ffn_quant(w):
            wsl = slice(512 * w, 512 * w + 512)
            for cc_ in range(CC):
                S.emit("scalar", lambda e, cc_=cc_, wsl=wsl: e.copy(
                    xbln[:, cc_ % 2, :], xs[:, cc_, wsl]),
                    reads=["xs%d" % cc_], writes=["xbln%d" % (cc_ % 2)], tag="ffn cast")
                S.emit("vector", lambda e, cc_=cc_: e.tensor_scalar(
                    out=xq_st[:, cc_, :], in0=xbln[:, cc_ % 2, :], scalar1=rinb[:, :],
                    scalar2=None, op0=OP.mult),
                    reads=["xbln%d" % (cc_ % 2), "rinb"], writes=["xq%d" % cc_],
                    tag="ffn quant")
            xqkeys = ["xq%d" % k for k in range(CC)]
            S.emit("scalar", lambda e: e.dma_start(
                out=xq_st[:, :, :].bitcast(I8), in_=xq_st[:, :, :].bitcast(I8)),
                reads=xqkeys, writes=xqkeys, is_dma=True, tag="xqsync")


        def ffn_w(w):
            wsl = slice(512 * w, 512 * w + 512)
            for m in range(CC):
                buf, bkey = (ping, "ping") if m % 2 == 0 else (pong, "pong")
                def f(e, m=m, buf=buf):
                    for k in range(CC):
                        inst = e.matmul(buf[:, 0:512], kq8[:, k, 128 * m:128 * m + 128],
                                        xq_st[:, k, :],
                                        start=(k == 0), stop=(k == CC - 1))
                    return inst
                S.emit("tensor", f, reads=["kq8"] + ["xq%d" % k for k in range(CC)],
                       writes=[bkey], tag="ffn mm m%d w%d" % (m, w))
                S.emit("scalar", lambda e, m=m, buf=buf: e.activation(
                    out=tst[:, m % 2, :], in_=buf[:, 0:512], func=AF.Identity,
                    scale=sscale[:, :], bias=biasf[:, m:m + 1]),
                    reads=[bkey, "sscale", "biasf"], writes=["tst%d" % (m % 2)],
                    tag="ffn ev")
                def f(e, m=m, wsl=wsl):
                    return e.tensor_tensor(
                        xs[:, m, wsl], xs[:, m, wsl], tst[:, m % 2, :], op=OP.add)
                S.emit("gpsimd" if m % 2 else "vector", f,
                       reads=["tst%d" % (m % 2), "xs%d" % m],
                       writes=["xs%d" % m], tag="ffn add")

        LN1B = (rsps[:, 0:512], auxps[:, 0:512], "rsps", "auxps")
        LN2B = (ctxps[:, 0, 0:512], ctxps[:, 1, 0:512], "ctx0", "ctx1")
        ln_stats_mm("ln1", 0, *LN1B)
        for w in range(NW + 1):
            if w < NW:
                ln_evac("ln1", *LN1B)
                if w + 1 < NW:
                    ln_stats_mm("ln1", w + 1, *LN1B)
            if w >= 1:
                ffn_w(w - 1)
                ln_stats_mm("ln2", w - 1, *LN2B)
            if w < NW:
                ln_chain("ln1", w)
                ln_norm("ln1", w)
                ffn_quant(w)
            if w >= 1:
                ln_evac("ln2", *LN2B)
                ln_chain("ln2", w - 1)
                ln_norm("ln2", w - 1)
                for cc_ in range(CC):
                    def f(e, cc_=cc_, w=w):
                        return e.dma_start(
                            out=d_out[128 * cc_:128 * cc_ + 128,
                                      512 * (w - 1):512 * w],
                            in_=xs[:, cc_, 512 * (w - 1):512 * w])
                    S.emit("sync", f, reads=["xs%d" % cc_], is_dma=True,
                           tag="st w%d c%d" % (w - 1, cc_))

        with nc.Block() as block:
            S.finalize(nc, block, sems)
    return nc


_NC_CACHE = {}

def _get_program(T):
    if T not in _NC_CACHE:
        _NC_CACHE[T] = build_program(T)
    return _NC_CACHE[T]


def make_in_map(xT, W_dq, W_uq, W_dkv, W_uk, W_uv, W_o, Wk, s_in, s_k, fbias):
    bf = ml_dtypes.bfloat16
    eye = np.eye(128, dtype=bf)
    eyef = np.eye(128, dtype=np.float32)
    mask = np.where(np.arange(128)[None, :] <= np.arange(128)[:, None],
                    np.float32(0.0), np.float32(NEG)).astype(bf)
    onesm_h = np.ones((128, 128), dtype=bf)
    onesr_h = np.ones((1, 128), dtype=np.float32)
    return {
        "xT": np.ascontiguousarray(xT, dtype=np.float32),
        "W_dq": np.asarray(W_dq, dtype=np.float32),
        "W_uq": np.asarray(W_uq, dtype=np.float32),
        "W_dkv": np.asarray(W_dkv, dtype=np.float32),
        "W_uk": np.asarray(W_uk, dtype=np.float32),
        "W_uv": np.asarray(W_uv, dtype=np.float32),
        "W_o": np.asarray(W_o, dtype=np.float32),
        "Wk": np.asarray(Wk).astype(bf),
        "s_in": np.ascontiguousarray(np.broadcast_to(
            np.asarray(s_in, dtype=np.float32).reshape(1, 1), (128, 1))),
        "s_k": np.ascontiguousarray(np.broadcast_to(
            np.asarray(s_k, dtype=np.float32).reshape(1, 1), (128, 1))),
        "fbias": np.ascontiguousarray(
            np.asarray(fbias).astype(bf).reshape(-1, 128).T),
        "eye128": eye, "eyef128": eyef, "maskqk": mask,
        "onesm": onesm_h, "onesrow": onesr_h,
    }


def kernel(L0_attn_x, L0_attn_W_dq, L0_attn_W_uq, L0_attn_W_dkv, L0_attn_W_uk,
           L0_attn_W_uv, L0_attn_W_o, L0_ffn_kernel, L0_ffn_input_scale,
           L0_ffn_kernel_scale, L0_ffn_bias):
    x = np.asarray(L0_attn_x, dtype=np.float32)
    B, T, C_ = x.shape
    nc = _get_program(T)
    in_maps = []
    for b in range(B):
        in_maps.append(make_in_map(
            x[b].T, L0_attn_W_dq, L0_attn_W_uq, L0_attn_W_dkv, L0_attn_W_uk,
            L0_attn_W_uv, L0_attn_W_o, L0_ffn_kernel, L0_ffn_input_scale,
            L0_ffn_kernel_scale, L0_ffn_bias))
    res = run_bass_kernel_spmd(nc, in_maps, core_ids=list(range(B)))
    out = np.stack([np.ascontiguousarray(r["out"].T) for r in res.results], axis=0)
    return out.astype(np.float32)
